# version 10
# baseline (speedup 1.0000x reference)
import numpy as np
import concourse.bass as bass
import concourse.mybir as mybir
from concourse.bass_utils import run_bass_kernel_spmd

F32 = mybir.dt.float32
BF16 = mybir.dt.bfloat16
AF = mybir.ActivationFunctionType
ALU = mybir.AluOpType
AX = mybir.AxisListType

L = 2048
D = 1024
H = 16
DH = 64
NT = 16
LN32 = -3.4657359027997265


class Sem:
    __slots__ = ("h", "n")

    def __init__(self, h):
        self.h = h
        self.n = 0

    def inc(self, k=1):
        self.n += k
        return self.n


class B:

    def __init__(self, nc):
        self.nc = nc
        self.ops = []
        self.cms = {}
        self.waited = {}

    def wait(self, eng, sem, val):
        if not val or val <= 0:
            return
        key = (eng, id(sem))
        if self.waited.get(key, 0) >= val:
            return
        self.waited[key] = val
        self.ops.append((eng, "w", (sem.h, val)))

    def op(self, eng, name, args, kwargs=None, inc=None, inck=1):
        n = inc.inc(inck) if inc is not None else None
        self.ops.append((eng, "o", (name, args, kwargs or {},
                                    inc.h if inc else None, inck)))
        return n

    def sb(self, name, shape, dtype, side="left"):
        cm = self.nc.sbuf_tensor(name, shape, dtype, side=side)
        h = cm.__enter__()
        self.cms[name] = cm
        return h

    def ps(self, name, shape, dtype):
        cm = self.nc.psum_tensor(name, shape, dtype)
        h = cm.__enter__()
        self.cms[name] = cm
        return h

    def free(self, *names):
        for n in names:
            self.cms.pop(n).__exit__(None, None, None)

    def replay(self, eng_obj, eng):
        for e, kind, p in self.ops:
            if e != eng:
                continue
            if kind == "w":
                eng_obj.wait_ge(*p)
            else:
                name, args, kwargs, sem, inck = p
                inst = getattr(eng_obj, name)(*args, **kwargs)
                if sem is not None:
                    inst.then_inc(sem, inck)


def build_core(dbg=(), upto="full"):
    nc = bass.Bass("TRN2", target_bir_lowering=False)
    bd = B(nc)
    dbg = set(dbg)

    x_h = nc.dram_tensor("x", [L, D], F32, kind="ExternalInput")
    wqk_h = nc.dram_tensor("wqk", [D, 2 * D], F32, kind="ExternalInput")
    wv_h = nc.dram_tensor("wv", [D, D], F32, kind="ExternalInput")
    wo_h = nc.dram_tensor("wo", [D, D], F32, kind="ExternalInput")
    wg_h = nc.dram_tensor("wg", [D, 8], F32, kind="ExternalInput")
    fa_h = nc.dram_tensor("fa", [D, 128], F32, kind="ExternalInput")
    fb_h = nc.dram_tensor("fb", [128, D], F32, kind="ExternalInput")
    qs_h = nc.dram_tensor("qs", [1, DH], F32, kind="ExternalInput")
    qb_h = nc.dram_tensor("qb", [1, DH], F32, kind="ExternalInput")
    ks_h = nc.dram_tensor("ks", [1, DH], F32, kind="ExternalInput")
    kb_h = nc.dram_tensor("kb", [1, DH], F32, kind="ExternalInput")
    sel_h = nc.dram_tensor("sel16", [16, 1024], F32, kind="ExternalInput")
    o_h = nc.dram_tensor("o", [L, D], F32, kind="ExternalOutput")
    x_ap = x_h.ap()
    o_ap = o_h.ap()
    dbg_d = {}
    for name, shape, dt in [("qkn", [128, NT * 2048], BF16),
                            ("vbuf", [128, NT * 1040], BF16),
                            ("aot", [128, 8 * 2048], BF16),
                            ("x1", [128, NT * 1024], F32),
                            ("lnT", [128, 8 * 2048], BF16)]:
        if name in dbg:
            dbg_d[name] = nc.dram_tensor("dbg_" + name, shape, dt,
                                         kind="ExternalOutput").ap()

    with (
        nc.semaphore("tt") as _tt,
        nc.semaphore("vv") as _vv,
        nc.semaphore("aa") as _aa,
        nc.semaphore("gg") as _gg,
        nc.semaphore("sy") as _sy,
        nc.semaphore("sc") as _sc,
        nc.semaphore("gq") as _gq,
        nc.Block() as block,
    ):
        tt, vv, aa, gg = Sem(_tt), Sem(_vv), Sem(_aa), Sem(_gg)
        sy, sc, gq = Sem(_sy), Sem(_sc), Sem(_gq)

        ident = bd.sb("ident", [128, 128], BF16)
        dmask = bd.sb("dmask", [128, 128], BF16)
        sel16 = bd.sb("sel16b", [16, 1024], BF16)
        s_qq = bd.sb("s_qq", [128, 1], F32)
        b_qq = bd.sb("b_qq", [128, 1], F32)
        s_kk = bd.sb("s_kk", [128, 1], F32)
        b_kk = bd.sb("b_kk", [128, 1], F32)
        eps5 = bd.sb("eps5", [128, 1], F32)
        eps6 = bd.sb("eps6", [128, 1], F32)
        ln32t = bd.sb("ln32t", [128, 1], F32)
        dnraw = bd.sb("dnraw", [1, 2048], BF16)
        dn_tok = bd.sb("dn_tok", [128, 256], BF16)
        dn_r = bd.sb("dn_r", [128, 256], F32)
        dn_rb = bd.sb("dn_rb", [128, 256], BF16)
        dn_r16 = bd.sb("dn_r16", [16, 2048], BF16)
        def sb2(name, shape, dtype):
            return [bd.sb(name + "A", shape, dtype),
                    bd.sb(name + "B", shape, dtype)]
        l_sum = sb2("l_sum", [128, 1], F32)
        l_s2 = sb2("l_s2", [128, 1], F32)
        l_m = sb2("l_m", [128, 1], F32)
        l_es = sb2("l_es", [128, 1], F32)
        l_m2 = sb2("l_m2", [128, 1], F32)
        l_var = sb2("l_var", [128, 1], F32)
        l_lnv = sb2("l_lnv", [128, 1], F32)
        l_rstd = sb2("l_rstd", [128, 1], F32)
        l_nmr = sb2("l_nmr", [128, 1], F32)
        st_sum = sb2("st_sum", [128, 16], F32)
        st_sq = sb2("st_sq", [128, 16], F32)
        st_m = sb2("st_m", [128, 16], F32)
        st_es = sb2("st_es", [128, 16], F32)
        st_m2 = sb2("st_m2", [128, 16], F32)
        st_var = sb2("st_var", [128, 16], F32)
        st_lnv = sb2("st_lnv", [128, 16], F32)
        st_r32 = sb2("st_r32", [128, 16], F32)
        st_nmr = sb2("st_nmr", [128, 16], F32)
        sqqk = sb2("sqqk2", [128, 1024], BF16)
        sqln_s = bd.sb("sqln", [128, 1024], F32)
        sqln2 = [sqln_s, sqln_s]
        glog = sb2("glog", [128, 8], F32)
        geq1 = sb2("geq1", [128, 8], F32)
        geq2 = sb2("geq2", [128, 8], F32)
        gt1 = sb2("gt1", [128, 8], F32)
        gmsk = sb2("gmsk", [128, 8], F32)
        gcw = sb2("gcw", [128, 8], F32)
        gc1 = sb2("gc1", [128, 8], F32)
        gc2 = sb2("gc2", [128, 8], F32)
        gm1 = sb2("gm1", [128, 1], F32)
        gm2 = sb2("gm2", [128, 1], F32)
        gd = sb2("gd", [128, 1], F32)
        ge = sb2("ge", [128, 1], F32)
        ge1 = sb2("ge1", [128, 1], F32)
        gw1 = sb2("gw1", [128, 1], F32)
        gw2 = sb2("gw2", [128, 1], F32)
        xbuf = bd.sb("xbuf", [128, 2 * 1024], F32)
        x_bf = bd.sb("x_bf", [128, 2 * 1024], BF16)
        lntok = bd.sb("lntok", [128, 2 * 1024], BF16)
        qkraw = bd.sb("qkraw", [128, 2 * 1024], BF16)

        qkr = bd.sb("qkr", [128, 2 * 1024], BF16)
        qkT_all = bd.sb("qkT_all", [128, NT * 2048], BF16)
        vbuf = bd.sb("vbuf", [128, NT * 1040], BF16)
        lnT = bd.sb("lnT", [128, 8 * 2048], BF16, side="right")
        wqh = bd.sb("wqh", [128, 8 * 1024], BF16, side="right")
        wvb = bd.sb("wvb", [128, 8 * 1024], BF16, side="right")

        ptpA = bd.ps("ptpA", [128, 1024], BF16)
        ptpB = bd.ps("ptpB", [128, 1024], BF16)
        pqkA = bd.ps("pqkA", [128, 1024], F32)
        pqkB = bd.ps("pqkB", [128, 1024], F32)
        pv = bd.ps("pv", [128, 1024], F32)
        ptp = [ptpA, ptpB]

        bd.op("g", "memset", (ident[:], 0.0))
        bd.op("g", "affine_select", (), dict(
            out=ident[:], in_=ident[:], compare_op=ALU.not_equal, fill=1.0,
            base=0, pattern=[[-1, 128]], channel_multiplier=1))
        bd.op("g", "memset", (dmask[:], 1.0))
        bd.op("g", "affine_select", (), dict(
            out=dmask[:], in_=dmask[:], compare_op=ALU.is_ge, fill=0.0,
            base=0, pattern=[[1, 128]], channel_multiplier=-1))
        bd.op("g", "dma_start", (sel16[:], sel_h.ap()), inc=gq, inck=16)
        bd.op("g", "memset", (eps5[:], 1e-5))
        bd.op("g", "memset", (eps6[:], 1e-6))
        bd.op("g", "memset", (ln32t[:], LN32))
        bd.op("g", "memset",
              (bass.AP(vbuf, 64, [[16640, 128], [1040, 16], [65, 16]]), 1024.0),
              inc=gg)
        g_setup = gg.n

        def wq_src(col_lo):
            return bass.AP(wqk_h, col_lo,
                           [[2048, 128], [2048 * 128, 8], [1, 1024]])
        wdst8 = bass.AP(wqh, 0, [[8192, 128], [1024, 8], [1, 1024]])
        bd.op("g", "dma_start", (wdst8, wq_src(0)), inc=gq, inck=16)
        d_wq_q = gq.n
        bd.op("g", "dma_start",
              (bass.AP(wvb, 0, [[8192, 128], [1024, 8], [1, 1024]]),
               bass.AP(wv_h, 0, [[1024, 128], [1024 * 128, 8], [1, 1024]])),
              inc=gq, inck=16)
        d_wv = gq.n
        for vh, dst in [(qs_h, s_qq), (qb_h, b_qq), (ks_h, s_kk), (kb_h, b_kk)]:
            bd.op("g", "dma_start", (dst[0:64, 0:1], vh.ap()), inc=gq, inck=16)
            bd.op("g", "dma_start", (dst[64:128, 0:1], vh.ap()), inc=gq, inck=16)

        tpc = [0, []]
        lntok_tp = [0, 0]
        d_x = [0] * NT
        xbf_vv = [0] * NT
        xsq_aa = [0] * NT
        ln_red = [0] * NT
        ln_var = [0] * NT
        ln_rstd = [0] * NT
        ln_nvv = [0] * NT
        lnT_ev = [0] * NT

        def ln_s0(t, src, is_x1):
            par = t % 2
            if not is_x1:
                if t >= 2:
                    bd.wait("s", aa, xbf_vv[t - 2])
                    bd.wait("s", aa, xsq_aa[t - 2])
                bd.op("s", "dma_start",
                      (xbuf[:, par * 1024:(par + 1) * 1024],
                       x_ap[t * 128:(t + 1) * 128, :]), inc=sy, inck=16)
                d_x[t] = sy.n
                bd.wait("v", sy, d_x[t])
                bd.wait("a", sy, d_x[t])
            bd.op("v", "tensor_reduce", (l_sum[par][:], src, AX.X, ALU.add),
                  inc=vv)
            ln_red[t] = vv.n
            bd.op("a", "activation", (sqln2[par][:], src, AF.Square),
                  dict(accum_out=l_s2[par][:]), inc=aa)
            if t >= 2 and ln_nvv[t - 2]:
                bd.wait("a", vv, ln_nvv[t - 2])
            bd.op("a", "activation",
                  (x_bf[:, par * 1024:(par + 1) * 1024], src, AF.Copy), inc=aa)
            xsq_aa[t] = aa.n
            xbf_vv[t] = aa.n

        def ln_s1(t, eps_ap):
            par = t % 2
            bd.wait("v", aa, xsq_aa[t])
            bd.wait("v", vv, ln_red[t])
            bd.op("v", "tensor_scalar",
                  (l_m[par][:], l_sum[par][:], 1.0 / 1024, None),
                  dict(op0=ALU.mult), inc=vv)
            bd.wait("v", vv, vv.n)
            bd.op("v", "tensor_scalar",
                  (l_es[par][:], l_s2[par][:], 1.0 / 1024, None),
                  dict(op0=ALU.mult), inc=vv)
            bd.wait("v", vv, vv.n)
            bd.op("v", "tensor_scalar",
                  (l_m2[par][:], l_m[par][:], l_m[par][:], None),
                  dict(op0=ALU.mult), inc=vv)
            bd.wait("v", vv, vv.n)
            bd.op("v", "tensor_tensor", (l_var[par][:], l_es[par][:], l_m2[par][:]),
                  dict(op=ALU.subtract), inc=vv)
            ln_var[t] = vv.n
            bd.wait("a", vv, ln_var[t])
            bd.op("a", "activation", (l_lnv[par][:], l_var[par][:], AF.Ln),
                  dict(bias=eps_ap), inc=aa)
            bd.wait("a", aa, aa.n)
            bd.op("a", "activation", (l_rstd[par][:], l_lnv[par][:], AF.Exp),
                  dict(scale=-0.5), inc=aa)
            ln_rstd[t] = aa.n

        def ln_s2(t, lnT_dst):
            par = t % 2
            bd.wait("v", aa, ln_rstd[t])
            bd.op("v", "tensor_scalar",
                  (l_nmr[par][:], l_m[par][:], l_rstd[par][:], -1.0),
                  dict(op0=ALU.mult, op1=ALU.mult), inc=vv)
            bd.wait("v", vv, vv.n)
            bd.wait("v", tt, lntok_tp[par])
            bd.op("v", "tensor_scalar",
                  (lntok[:, par * 1024:(par + 1) * 1024],
                   x_bf[:, par * 1024:(par + 1) * 1024],
                   l_rstd[par][:], l_nmr[par][:]),
                  dict(op0=ALU.mult, op1=ALU.add), inc=vv)
            v_ln = vv.n
            ln_nvv[t] = v_ln
            bd.wait("t", vv, v_ln)
            for c in range(8):
                p = ptp[tpc[0] % 2]
                if tpc[0] >= 2:
                    ge_, gn_ = tpc[1][tpc[0] - 2]
                    bd.wait("t", vv if ge_ == "v" else aa, gn_)
                bd.op("t", "transpose",
                      (p[:, 0:128],
                       lntok[:, par * 1024 + c * 128: par * 1024 + c * 128 + 128],
                       ident[:]), inc=tt)
                ntp = tt.n
                bd.wait("a", tt, ntp)
                bd.op("a", "activation",
                      (lnT_dst[:, c * 2048 + t * 128: c * 2048 + t * 128 + 128],
                       p[:, 0:128], AF.Copy), inc=aa)
                tpc[1].append(("a", aa.n))
                tpc[0] += 1
            lntok_tp[par] = tt.n
            lnT_ev[t] = ("a", aa.n)

        bd.wait("t", gg, g_setup)
        bd.wait("v", gg, g_setup)
        bd.wait("a", gg, g_setup)
        for i in range(NT + 2):
            if i >= 2:
                ln_s2(i - 2, lnT)
            if 1 <= i <= NT:
                ln_s1(i - 1, eps5[:])
            if i < NT:
                ln_s0(i, xbuf[:, (i % 2) * 1024:(i % 2 + 1) * 1024], False)

        qkn_done = [0] * (2 * NT)
        qkraw_free = [None, None]
        qkr_tp = [0, 0]
        pq_free = {0: [], 1: []}
        vbuf_ev = [0] * NT
        pv_free = []
        u_mm = {}
        u_araw = {}
        u_r32 = {}

        units = ([("q", t) for t in range(NT)] + [("v", t) for t in range(NT)]
                 + [("k", t) for t in range(NT)])

        def proj_m(u):
            kind, t = u
            if kind == "v":
                bd.wait("t", gq, d_wv)
                bd.wait("t", aa, lnT_ev[t][1])
                for sem, val in pv_free:
                    bd.wait("t", sem, val)
                wsrc, ps_ = wvb, pv
            else:
                bd.wait("t", gq, d_wq_q if kind == "q" else d_wq_k_val[0])
                bd.wait("t", aa, lnT_ev[t][1])
                for sem, val in pq_free[t % 2]:
                    bd.wait("t", sem, val)
                wsrc, ps_ = wqh, (pqkA if t % 2 == 0 else pqkB)
            for f in range(8):
                for cc in range(2):
                    bd.op("t", "matmul",
                          (ps_[:, cc * 512:(cc + 1) * 512],
                           lnT[:, f * 2048 + t * 128: f * 2048 + t * 128 + 128],
                           wsrc[:, f * 1024 + cc * 512: f * 1024 + (cc + 1) * 512]),
                          dict(start=(f == 0), stop=(f == 7)),
                          inc=(tt if (f == 7 and cc == 1) else None))
            u_mm[u] = tt.n

        def proj_s1(u):
            kind, t = u
            par = t % 2
            if kind == "v":
                bd.wait("v", tt, u_mm[u])
                bd.op("v", "tensor_copy",
                      (bass.AP(vbuf, t * 1040, [[16640, 128], [65, 16], [1, 64]]),
                       pv[:, 0:1024].rearrange("p (g d) -> p g d", d=64)), inc=vv)
                vbuf_ev[t] = vv.n
                pv_free[:] = [(vv, vv.n)]
                return
            half = 0 if kind == "q" else 1
            ps_ = pqkA if par == 0 else pqkB
            bd.wait("a", tt, u_mm[u])
            qf = qkraw_free[par]
            if qf:
                bd.wait("a", aa, qf[1])
                bd.wait("a", vv, qf[2])
            bd.op("a", "activation",
                  (qkraw[:, par * 1024:(par + 1) * 1024], ps_[:, 0:1024],
                   AF.Copy), inc=aa)
            a_raw = aa.n
            u_araw[u] = a_raw
            bd.wait("v", tt, u_mm[u])
            bd.wait("v", aa, a_raw)
            bd.op("v", "tensor_reduce",
                  (st_sum[par][:],
                   ps_[:, 0:1024].rearrange("p (g d) -> p g d", d=64),
                   AX.X, ALU.add), inc=vv)
            v_sums = vv.n
            bd.op("a", "activation",
                  (sqqk[par][:], qkraw[:, par * 1024:(par + 1) * 1024],
                   AF.Square), inc=aa)
            a_sq2 = aa.n
            bd.wait("v", aa, a_sq2)
            bd.op("v", "tensor_reduce",
                  (st_sq[par][:],
                   sqqk[par][:].rearrange("p (g d) -> p g d", d=64),
                   AX.X, ALU.add), inc=vv)
            pq_free[par] = [(vv, v_sums), (aa, a_raw)]
            bd.wait("v", vv, vv.n)
            bd.op("v", "tensor_scalar",
                  (st_m[par][:], st_sum[par][:], 1.0 / 2048.0, None),
                  dict(op0=ALU.mult), inc=vv)
            bd.wait("v", vv, vv.n)
            bd.op("v", "tensor_scalar",
                  (st_es[par][:], st_sq[par][:], 1.0 / 65536.0, None),
                  dict(op0=ALU.mult), inc=vv)
            bd.wait("v", vv, vv.n)
            bd.op("v", "tensor_tensor",
                  (st_m2[par][:], st_m[par][:], st_m[par][:]),
                  dict(op=ALU.mult), inc=vv)
            bd.wait("v", vv, vv.n)
            bd.op("v", "tensor_tensor",
                  (st_var[par][:], st_es[par][:], st_m2[par][:]),
                  dict(op=ALU.subtract), inc=vv)
            v_var = vv.n
            bd.wait("a", vv, v_var)
            bd.op("a", "activation", (st_lnv[par][:], st_var[par][:], AF.Ln),
                  dict(bias=eps6[:]), inc=aa)
            bd.wait("a", aa, aa.n)
            bd.op("a", "activation", (st_r32[par][:], st_lnv[par][:], AF.Exp),
                  dict(scale=-0.5, bias=ln32t[:]), inc=aa)
            u_r32[u] = aa.n

        def proj_s2(u):
            kind, t = u
            if kind == "v":
                return
            par = t % 2
            half = 0 if kind == "q" else 1
            bd.wait("v", aa, u_r32[u])
            bd.op("v", "tensor_tensor",
                  (st_nmr[par][:], st_m[par][:], st_r32[par][:]),
                  dict(op=ALU.mult), inc=vv)
            bd.wait("v", vv, vv.n)
            bd.op("v", "tensor_scalar",
                  (st_nmr[par][:], st_nmr[par][:], -32.0, None),
                  dict(op0=ALU.mult), inc=vv)
            bd.wait("v", vv, vv.n)
            bd.wait("v", tt, qkr_tp[par])
            for gix in range(16):
                bd.op("v", "tensor_scalar",
                      (qkr[:, par * 1024 + gix * 64: par * 1024 + gix * 64 + 64],
                       qkraw[:, par * 1024 + gix * 64: par * 1024 + gix * 64 + 64],
                       st_r32[par][:, gix:gix + 1], st_nmr[par][:, gix:gix + 1]),
                      dict(op0=ALU.mult, op1=ALU.add), inc=vv)
            v_norm = vv.n
            a_norm = 0
            qkraw_free[par] = ("v", 0, v_norm)
            sv_ = s_qq if half == 0 else s_kk
            bv_ = b_qq if half == 0 else b_kk
            bd.wait("t", vv, v_norm)
            bd.wait("t", aa, a_norm)
            for g in range(8):
                p = ptp[tpc[0] % 2]
                if tpc[0] >= 2:
                    ge_, gn_ = tpc[1][tpc[0] - 2]
                    bd.wait("t", vv if ge_ == "v" else aa, gn_)
                bd.op("t", "transpose",
                      (p[:, 0:128],
                       qkr[:, par * 1024 + g * 128: par * 1024 + g * 128 + 128],
                       ident[:]), inc=tt)
                ntp = tt.n
                bd.wait("a", tt, ntp)
                bd.op("a", "activation",
                      (qkT_all[:, (2 * g + half) * 2048 + t * 128:
                               (2 * g + half) * 2048 + t * 128 + 128],
                       p[:, 0:128], AF.Identity),
                      dict(scale=sv_[:], bias=bv_[:]), inc=aa)
                tpc[1].append(("a", aa.n))
                tpc[0] += 1
            qkr_tp[par] = tt.n
            qkn_done[t + half * NT] = ("a", aa.n, vv.n)

        d_wq_k_val = [0]
        for i in range(len(units) + 2):
            if i >= 2:
                proj_s2(units[i - 2])
            if 1 <= i <= len(units):
                proj_s1(units[i - 1])
            if i < len(units):
                if units[i] == ("k", 0):
                    bd.wait("g", tt, u_mm[("q", NT - 1)])
                    bd.op("g", "dma_start", (wdst8, wq_src(1024)),
                          inc=gq, inck=16)
                    d_wq_k_val[0] = gq.n
                proj_m(units[i])
        aa_passk = aa.n

        if "qkn" in dbg_d:
            bd.wait("s", vv, vv.n)
            bd.op("s", "dma_start", (dbg_d["qkn"], qkT_all[:]), inc=sy, inck=16)
        if "vbuf" in dbg_d:
            bd.wait("s", vv, vv.n)
            bd.op("s", "dma_start", (dbg_d["vbuf"], vbuf[:]), inc=sy, inck=16)
        if "lnT" in dbg_d:
            bd.wait("s", vv, vv.n)
            bd.op("s", "dma_start", (dbg_d["lnT"], lnT[:]), inc=sy, inck=16)

        if upto == "proj":
            bd.wait("s", sy, sy.n)
            bd.wait("g", sy, sy.n)
            return _finish(bd, block, nc)

        bd.free("wvb", "wqh", "lnT")
        bd.free("pv", "pqkB", "pqkA", "ptpB", "ptpA")
        aoT = bd.sb("aoT", [128, 8 * 2048], BF16, side="right")
        pTb = bd.sb("pTb", [128, 2 * 2048], BF16, side="right")
        repsb = bd.sb("repsb", [64, 2048], BF16, side="right")
        otmp = bd.sb("otmp", [64, 2048], BF16, side="right")
        repsb_hi = bd.sb("repsb_hi", [128, 2048], BF16, side="right")
        psA = bd.ps("psA", [128, 1024], F32)
        psB = bd.ps("psB", [128, 1024], F32)
        poutA = bd.ps("poutA", [128, 1024], F32)
        poutB = bd.ps("poutB", [128, 1024], F32)
        ps2 = [psA, psB]
        pout2 = [poutA, poutB]

        att_start_vv = vv.n
        aoT_parts = []
        pout_free_h = {0: [(vv, att_start_vv), (aa, aa_passk),
                           (vv, vbuf_ev[NT - 1])],
                       1: [(vv, att_start_vv), (aa, aa_passk),
                           (vv, vbuf_ev[NT - 1])]}
        ps_free_par = {0: [(vv, att_start_vv), (aa, aa_passk)],
                       1: [(vv, att_start_vv), (aa, aa_passk)]}
        exp_par = [0, 0]
        ptb_free = [0, 0, 0, 0]
        shift_prev = 0
        dnraw_dma = 0

        for g in range(8):
            hA, hB = 2 * g, 2 * g + 1
            qblk, kblk = 2 * g, 2 * g + 1
            for qh in range(2):
                qbase = 1024 * qh
                jmax = 8 * qh + 8
                pend = None
                av_last = 0

                def emit_av2(pend):
                    j, a0, n, off, aexps, mvvs = pend
                    last = 0
                    for hx in range(2):
                        h = 2 * g + hx
                        pout_ = pout2[hx]
                        bd.wait("t", aa, aexps[hx])
                        if mvvs[hx]:
                            bd.wait("t", vv, mvvs[hx])
                        if j == 0:
                            for sem, val in pout_free_h[hx]:
                                bd.wait("t", sem, val)
                            pout_free_h[hx] = []
                        w = off
                        while w < off + n:
                            we = min((w // 512 + 1) * 512, off + n)
                            bd.op("t", "matmul",
                                  (pout_[0:65, w:we],
                                   vbuf[:, j * 1040 + h * 65:
                                        j * 1040 + h * 65 + 65],
                                   pTb[:, (hx * 2 + j % 2) * 1024 + w:
                                        (hx * 2 + j % 2) * 1024 + we]),
                                  dict(start=(j == 0), stop=(j == jmax - 1)),
                                  inc=(tt if we == off + n else None))
                            w = we
                        ptb_free[hx * 2 + j % 2] = tt.n
                        last = tt.n
                    return last

                for j in range(jmax):
                    qlo_g = 128 * j
                    a0 = max(qbase, qlo_g)
                    n = qbase + 1024 - a0
                    off = a0 - qbase
                    diag = qlo_g >= qbase
                    aexps = [0, 0]
                    mvvs = [0, 0]
                    for hx in range(2):
                        base = hx * 64
                        ps_ = ps2[hx]
                        for sem, val in ps_free_par[hx]:
                            bd.wait("t", sem, val)
                        ps_free_par[hx] = []
                        if exp_par[hx]:
                            bd.wait("t", aa, exp_par[hx])
                        lhs_k = qkT_all[base:base + 64,
                                        kblk * 2048 + qlo_g:
                                        kblk * 2048 + qlo_g + 128]
                        w = off
                        while w < off + n:
                            we = min((w // 512 + 1) * 512, off + n)
                            bd.op("t", "matmul",
                                  (ps_[:, w:we], lhs_k,
                                   qkT_all[base:base + 64,
                                           qblk * 2048 + qbase + w:
                                           qblk * 2048 + qbase + we]),
                                  dict(start=True, stop=True),
                                  inc=(tt if we == off + n else None))
                            w = we
                        s_done = tt.n
                        preg = (hx * 2 + j % 2) * 1024
                        bd.wait("a", tt, s_done)
                        bd.wait("a", tt, ptb_free[hx * 2 + j % 2])
                        bd.op("a", "activation",
                              (pTb[:, preg + off: preg + off + n],
                               ps_[:, off:off + n], AF.Exp),
                              dict(scale=1.0 / 64), inc=aa)
                        aexps[hx] = aa.n
                        exp_par[hx] = aa.n
                        if diag:
                            bd.wait("v", aa, aexps[hx])
                            bd.op("v", "tensor_tensor",
                                  (pTb[:, preg + off: preg + off + 128],
                                   pTb[:, preg + off: preg + off + 128],
                                   dmask[:]), dict(op=ALU.mult), inc=vv)
                            mvvs[hx] = vv.n
                    if pend is not None:
                        av_last = emit_av2(pend)
                    pend = (j, a0, n, off, aexps, mvvs)
                av_last = emit_av2(pend)

                bd.wait("a", tt, av_last)
                for hx in range(2):
                    h = 2 * g + hx
                    pout_ = pout2[hx]
                    if hx == 0:
                        raw_dst = aoT[0:64, g * 2048 + qbase:
                                      g * 2048 + qbase + 1024]
                    else:
                        bd.wait("a", sy, shift_prev)
                        raw_dst = otmp[0:64, 0:1024]
                    bd.wait("a", tt, av_last)
                    bd.op("a", "activation",
                          (raw_dst, pout_[0:64, 0:1024], AF.Copy), inc=aa)
                    a_raw0 = aa.n
                    bd.wait("a", sc, dnraw_dma)
                    bd.op("a", "activation",
                          (dnraw[0:1, 0:1024], pout_[64:65, 0:1024], AF.Copy),
                          inc=aa)
                    a_dn = aa.n
                    pout_free_h[hx] = [(aa, a_dn)]
                    bd.wait("a", aa, a_dn)
                    bd.op("a", "dma_start",
                          (dn_tok[8 * h + 4 * qh: 8 * h + 4 * qh + 4, :],
                           dnraw[0:1, 0:1024]), inc=sc, inck=16)
                    dnraw_dma = sc.n
                    if hx == 0:
                        aoT_parts.append((aa, a_raw0))
                    else:
                        bd.wait("s", aa, a_raw0)
                        bd.op("s", "dma_start",
                              (aoT[64:128, g * 2048 + qbase:
                                   g * 2048 + qbase + 1024],
                               otmp[0:64, 0:1024]), inc=sy, inck=16)
                        shift_prev = sy.n
                        aoT_parts.append((sy, sy.n))

        att_tt = tt.n

        bd.wait("v", sc, dnraw_dma)
        bd.op("v", "reciprocal", (dn_r[:], dn_tok[:]), inc=vv)
        bd.wait("v", vv, vv.n)
        bd.op("v", "tensor_copy", (dn_rb[:], dn_r[:]), inc=vv)
        v_dnrb = vv.n
        bd.wait("a", vv, v_dnrb)
        for h in range(H):
            bd.op("a", "dma_start",
                  (dn_r16[h:h + 1, :], dn_rb[8 * h: 8 * h + 8, :]),
                  inc=sc, inck=16)
        d_r16 = sc.n
        norm_done = []
        rep_free = []
        shift2_prev = 0
        for h in range(H):
            g = h // 2
            bd.wait("t", sc, d_r16)
            if h == 0:
                for kp in (0, 1):
                    for sem, val in ps_free_par[kp]:
                        bd.wait("t", sem, val)
                    if exp_par[kp]:
                        bd.wait("t", aa, exp_par[kp])
            for sem, val in rep_free:
                bd.wait("t", sem, val)
            rep_free = []
            for r in range(2):
                for w in range(2):
                    bd.op("t", "matmul",
                          (psA[0:64, w * 512:(w + 1) * 512],
                           sel16[0:16, 64 * h: 64 * h + 64],
                           dn_r16[0:16, (2 * r + w) * 512:(2 * r + w + 1) * 512]),
                          dict(start=True, stop=True),
                          inc=(tt if w == 1 else None))
                t_rep = tt.n
                bd.wait("v", tt, t_rep)
                if h % 2 == 1:
                    bd.wait("v", sy, shift2_prev)
                bd.op("v", "tensor_copy",
                      (repsb[0:64, r * 1024:(r + 1) * 1024], psA[0:64, 0:1024]),
                      inc=vv)
                v_reps = vv.n
                if r == 0:
                    bd.wait("t", vv, v_reps)
            rep_free = [(vv, v_reps)]
            if h % 2 == 0:
                bd.wait("v", aa, aoT_parts[h][1])
                bd.op("v", "tensor_tensor",
                      (aoT[0:64, g * 2048:(g + 1) * 2048],
                       aoT[0:64, g * 2048:(g + 1) * 2048],
                       repsb[0:64, 0:2048]), dict(op=ALU.mult), inc=vv)
                norm_done.append((vv, vv.n))
            else:
                bd.wait("s", vv, v_reps)
                if len(norm_done) >= 2:
                    bd.wait("s", norm_done[-2][0], norm_done[-2][1])
                bd.op("s", "dma_start",
                      (repsb_hi[64:128, 0:2048], repsb[0:64, 0:2048]),
                      inc=sy, inck=16)
                shift2_prev = sy.n
                bd.wait("v", sy, shift2_prev)
                bd.wait("v", sy, aoT_parts[h][1])
                bd.op("v", "tensor_tensor",
                      (aoT[64:128, g * 2048:(g + 1) * 2048],
                       aoT[64:128, g * 2048:(g + 1) * 2048],
                       repsb_hi[64:128, 0:2048]), dict(op=ALU.mult), inc=vv)
                norm_done.append((vv, vv.n))
                rep_free.append((sy, shift2_prev))
        aoT_parts = norm_done
        att_vv = vv.n

        if upto == "attn":
            if "aot" in dbg_d:
                for sem, val in aoT_parts:
                    bd.wait("s", sem, val)
                bd.op("s", "dma_start", (dbg_d["aot"], aoT[:]), inc=sy, inck=16)
            bd.wait("s", sy, sy.n)
            bd.wait("g", sy, sy.n)
            return _finish(bd, block, nc)

        bd.free("repsb_hi", "otmp", "repsb", "pTb", "vbuf", "qkT_all")
        bd.free("poutB", "poutA", "psB", "psA")
        wob = bd.sb("wob", [128, 8 * 1024], BF16, side="right")
        x1 = bd.sb("x1", [128, NT * 1024], F32)
        poA = bd.ps("poA", [128, 1024], F32)
        poB = bd.ps("poB", [128, 1024], F32)

        bd.wait("g", tt, att_tt)
        bd.wait("g", vv, att_vv)
        bd.wait("g", sy, max(shift_prev, shift2_prev))
        bd.op("g", "dma_start",
              (bass.AP(wob, 0, [[8192, 128], [1024, 8], [1, 1024]]),
               bass.AP(wo_h, 0, [[1024, 128], [1024 * 128, 8], [1, 1024]])),
              inc=gq, inck=16)
        d_wo = gq.n

        for sem, val in aoT_parts:
            bd.wait("t", sem, val)
        bd.wait("t", gq, d_wo)
        bd.wait("t", aa, aa.n)
        x1_done = [0] * NT
        po_free = {0: [], 1: []}
        d_x2 = [0] * NT
        for t in range(NT):
            po = poA if t % 2 == 0 else poB
            if t >= 2:
                bd.wait("s", vv, x1_done[t - 2])
            bd.op("s", "dma_start",
                  (xbuf[:, (t % 2) * 1024:(t % 2 + 1) * 1024],
                   x_ap[t * 128:(t + 1) * 128, :]), inc=sy, inck=16)
            d_x2[t] = sy.n
            for sem, val in po_free[t % 2]:
                bd.wait("t", sem, val)
            for f in range(8):
                for cc in range(2):
                    bd.op("t", "matmul",
                          (po[:, cc * 512:(cc + 1) * 512],
                           aoT[:, f * 2048 + t * 128: f * 2048 + t * 128 + 128],
                           wob[:, f * 1024 + cc * 512: f * 1024 + (cc + 1) * 512]),
                          dict(start=(f == 0), stop=(f == 7)),
                          inc=(tt if (f == 7 and cc == 1) else None))
            t_po = tt.n
            bd.wait("v", tt, t_po)
            bd.wait("v", sy, d_x2[t])
            bd.op("v", "tensor_tensor",
                  (x1[:, t * 1024:(t + 1) * 1024], po[:, 0:1024],
                   xbuf[:, (t % 2) * 1024:(t % 2 + 1) * 1024]),
                  dict(op=ALU.add), inc=vv)
            x1_done[t] = vv.n
            po_free[t % 2] = [(vv, vv.n)]
        t_po_last = tt.n

        if "x1" in dbg_d:
            bd.wait("s", vv, x1_done[NT - 1])
            bd.op("s", "dma_start", (dbg_d["x1"], x1[:]), inc=sy, inck=16)
        if "aot" in dbg_d:
            for sem, val in aoT_parts:
                bd.wait("s", sem, val)
            bd.op("s", "dma_start", (dbg_d["aot"], aoT[:]), inc=sy, inck=16)

        bd.free("wob", "aoT")
        bd.free("poB", "poA")
        ptpA2 = bd.ps("ptpA2", [128, 1024], BF16)
        ptpB2 = bd.ps("ptpB2", [128, 1024], BF16)
        ptp[0] = ptpA2
        ptp[1] = ptpB2
        lnT2 = bd.sb("lnT2", [128, 8 * 2048], BF16, side="right")
        wgb = bd.sb("wgb", [128, 64], BF16, side="right")
        fab = bd.sb("fab", [128, 8 * 128], BF16, side="right")
        fbb = bd.sb("fbb", [128, 1024], BF16, side="right")
        hsbA = bd.sb("hsbA", [128, 128], BF16, side="right")
        hsbB = bd.sb("hsbB", [128, 128], BF16, side="right")
        hsb = [hsbA, hsbB]
        hwb = bd.sb("hwb", [128, 128], BF16, side="right")
        hwT = bd.sb("hwT", [128, 128], BF16, side="right")
        outst = bd.sb("outst", [128, 2 * 1024], F32, side="right")
        pg = bd.ps("pg", [128, 512], F32)
        ph = bd.ps("ph", [128, 512], F32)
        pfoA = bd.ps("pfoA", [128, 1024], F32)
        pfoB = bd.ps("pfoB", [128, 1024], F32)

        bd.wait("g", tt, t_po_last)
        bd.op("g", "dma_start",
              (bass.AP(wgb, 0, [[64, 128], [8, 8], [1, 8]]),
               bass.AP(wg_h, 0, [[8, 128], [1024, 8], [1, 8]])),
              inc=gq, inck=16)
        bd.op("g", "dma_start",
              (bass.AP(fab, 0, [[1024, 128], [128, 8], [1, 128]]),
               bass.AP(fa_h, 0, [[128, 128], [128 * 128, 8], [1, 128]])),
              inc=gq, inck=16)
        bd.op("g", "dma_start", (fbb[:], fb_h.ap()), inc=gq, inck=16)
        d_ffnw = gq.n

        gate_free = []
        pfo_free = {0: [], 1: []}
        hw_guard = [0]
        d_out = [0] * NT
        t_pg_l = [0] * NT
        t_ph_l = [0] * NT
        a_gl_l = [0] * NT
        a_ge_l = [0] * NT
        a_ph_l = [0] * NT
        gd_vv_l = [0] * NT
        hwb_vv_l = [0] * NT
        bd.wait("t", vv, x1_done[NT - 1])

        def ffn_f0(t):
            bd.wait("v", vv, x1_done[t])
            bd.wait("a", vv, x1_done[t])
            ln_s0(t, x1[:, t * 1024:(t + 1) * 1024], True)

        def ffn_f3(t):
            gp = t % 2
            bd.wait("t", gq, d_ffnw)
            bd.wait("t", aa, lnT_ev[t][1])
            if t >= 1:
                bd.wait("t", aa, a_gl_l[t - 1])
                bd.wait("t", aa, a_ph_l[t - 1])
            for f in range(8):
                bd.op("t", "matmul",
                      (pg[:, 0:8],
                       lnT2[:, f * 2048 + t * 128: f * 2048 + t * 128 + 128],
                       wgb[:, f * 8: f * 8 + 8]),
                      dict(start=(f == 0), stop=(f == 7)),
                      inc=(tt if f == 7 else None))
            t_pg_l[t] = tt.n
            for f in range(8):
                bd.op("t", "matmul",
                      (ph[:, 0:128],
                       lnT2[:, f * 2048 + t * 128: f * 2048 + t * 128 + 128],
                       fab[:, f * 128: f * 128 + 128]),
                      dict(start=(f == 0), stop=(f == 7)),
                      inc=(tt if f == 7 else None))
            t_ph_l[t] = tt.n

        def ffn_f4a(t):
            gp = t % 2
            bd.wait("a", tt, t_pg_l[t])
            if t >= 2:
                bd.wait("a", vv, gd_vv_l[t - 2])
            bd.op("a", "activation", (glog[gp][:], pg[:, 0:8], AF.Copy), inc=aa)
            a_gl_l[t] = aa.n
            bd.wait("a", tt, t_ph_l[t])
            if t >= 2:
                bd.wait("a", vv, hwb_vv_l[t - 2])
            bd.op("a", "activation", (hsb[gp][:], ph[:, 0:128], AF.Copy), inc=aa)
            a_ph_l[t] = aa.n

        def ffn_f4b(t):
            gp = t % 2
            bd.wait("v", aa, a_gl_l[t])
            bd.op("v", "tensor_reduce", (gm1[gp][:], glog[gp][:], AX.X, ALU.max),
                  inc=vv)
            bd.wait("v", vv, vv.n)
            bd.op("v", "tensor_scalar", (geq1[gp][:], glog[gp][:], gm1[gp][:], None),
                  dict(op0=ALU.is_equal), inc=vv)
            bd.wait("v", vv, vv.n)
            bd.op("v", "tensor_scalar", (gt1[gp][:], geq1[gp][:], 1e9, None),
                  dict(op0=ALU.mult), inc=vv)
            bd.wait("v", vv, vv.n)
            bd.op("v", "tensor_tensor", (gmsk[gp][:], glog[gp][:], gt1[gp][:]),
                  dict(op=ALU.subtract), inc=vv)
            bd.wait("v", vv, vv.n)
            bd.op("v", "tensor_reduce", (gm2[gp][:], gmsk[gp][:], AX.X, ALU.max),
                  inc=vv)
            bd.wait("v", vv, vv.n)
            bd.op("v", "tensor_scalar", (geq2[gp][:], glog[gp][:], gm2[gp][:], None),
                  dict(op0=ALU.is_equal), inc=vv)
            bd.wait("v", vv, vv.n)
            bd.op("v", "tensor_tensor", (gd[gp][:], gm2[gp][:], gm1[gp][:]),
                  dict(op=ALU.subtract), inc=vv)
            v_gd = vv.n
            gd_vv_l[t] = v_gd
            bd.wait("a", vv, v_gd)
            bd.op("a", "activation", (ge[gp][:], gd[gp][:], AF.Exp), inc=aa)
            a_ge_l[t] = aa.n

        def ffn_f5(t):
            gp = t % 2
            bd.wait("v", aa, a_ge_l[t])
            bd.op("v", "tensor_scalar", (ge1[gp][:], ge[gp][:], 1.0, None),
                  dict(op0=ALU.add), inc=vv)
            bd.wait("v", vv, vv.n)
            bd.op("v", "reciprocal", (gw1[gp][:], ge1[gp][:]), inc=vv)
            bd.wait("v", vv, vv.n)
            bd.op("v", "tensor_scalar", (gw2[gp][:], gw1[gp][:], -1.0, 1.0),
                  dict(op0=ALU.mult, op1=ALU.add), inc=vv)
            bd.wait("v", vv, vv.n)
            bd.op("v", "tensor_scalar", (gc1[gp][:], geq1[gp][:], gw1[gp][:], None),
                  dict(op0=ALU.mult), inc=vv)
            bd.wait("v", vv, vv.n)
            bd.op("v", "tensor_scalar", (gc2[gp][:], geq2[gp][:], gw2[gp][:], None),
                  dict(op0=ALU.mult), inc=vv)
            bd.wait("v", vv, vv.n)
            bd.op("v", "tensor_tensor", (gcw[gp][:], gc1[gp][:], gc2[gp][:]),
                  dict(op=ALU.add), inc=vv)
            bd.wait("v", aa, a_ph_l[t])
            bd.wait("v", vv, vv.n)
            bd.wait("v", tt, hw_guard[0])
            for e in range(8):
                bd.op("v", "tensor_scalar",
                      (hwb[:, e * 16:(e + 1) * 16],
                       hsb[gp][:, e * 16:(e + 1) * 16],
                       gcw[gp][:, e:e + 1], None), dict(op0=ALU.mult), inc=vv)
            hwb_vv_l[t] = vv.n

        def ffn_f6(t):
            p = ptp[tpc[0] % 2]
            if tpc[0] >= 2:
                ge_, gn_ = tpc[1][tpc[0] - 2]
                bd.wait("t", vv if ge_ == "v" else aa, gn_)
            bd.wait("t", vv, hwb_vv_l[t])
            bd.op("t", "transpose", (p[:, 0:128], hwb[:], ident[:]), inc=tt)
            t_hwT = tt.n
            bd.wait("v", tt, t_hwT)
            bd.op("v", "tensor_copy", (hwT[:], p[:, 0:128]), inc=vv)
            v_hwT = vv.n
            tpc[1].append(("v", vv.n))
            tpc[0] += 1
            pfo = pfoA if t % 2 == 0 else pfoB
            bd.wait("t", vv, v_hwT)
            for sem, val in pfo_free[t % 2]:
                bd.wait("t", sem, val)
            for cc in range(2):
                bd.op("t", "matmul",
                      (pfo[:, cc * 512:(cc + 1) * 512], hwT[:],
                       fbb[:, cc * 512:(cc + 1) * 512]),
                      dict(start=True, stop=True),
                      inc=(tt if cc == 1 else None))
            t_fo = tt.n
            hw_guard[0] = t_fo
            bd.wait("v", tt, t_fo)
            if t >= 2:
                bd.wait("v", sy, d_out[t - 2])
            bd.op("v", "tensor_tensor",
                  (outst[:, (t % 2) * 1024:(t % 2 + 1) * 1024],
                   pfo[:, 0:1024], x1[:, t * 1024:(t + 1) * 1024]),
                  dict(op=ALU.add), inc=vv)
            v_out = vv.n
            pfo_free[t % 2] = [(vv, v_out)]
            bd.wait("s", vv, v_out)
            bd.op("s", "dma_start",
                  (o_ap[t * 128:(t + 1) * 128, :],
                   outst[:, (t % 2) * 1024:(t % 2 + 1) * 1024]),
                  inc=sy, inck=16)
            d_out[t] = sy.n

        for i in range(NT + 4):
            if i >= 4:
                ffn_f5(i - 4)
                ffn_f6(i - 4)
            if i >= 3 and i - 3 < NT:
                ffn_f4b(i - 3)
            if i >= 2 and i - 2 < NT:
                ln_s2(i - 2, lnT2)
                ffn_f3(i - 2)
                ffn_f4a(i - 2)
            if 1 <= i <= NT:
                ln_s1(i - 1, eps5[:])
            if i < NT:
                ffn_f0(i)

        bd.wait("s", sy, d_out[NT - 1])
        bd.wait("g", sy, d_out[NT - 1])
        return _finish(bd, block, nc)


def _finish(bd, block, nc):
    @block.tensor
    def _(e):
        bd.replay(e, "t")

    @block.vector
    def _(e):
        bd.replay(e, "v")

    @block.scalar
    def _(e):
        bd.replay(e, "a")

    @block.gpsimd
    def _(e):
        bd.replay(e, "g")

    @block.sync
    def _(e):
        bd.replay(e, "s")

    return nc


def make_inmaps(L0_attn_inputs, L0_attn_W_qk, L0_attn_W_v, L0_attn_W_out,
                L0_attn_q_scale, L0_attn_q_bias, L0_attn_k_scale,
                L0_attn_k_bias, L0_ffn_Wg, L0_ffn_A, L0_ffn_B):
    wqk_p = np.ascontiguousarray(
        np.asarray(L0_attn_W_qk).reshape(D, H, 2, DH).transpose(0, 2, 1, 3)
        .reshape(D, 2 * D))
    fa_r = np.ascontiguousarray(np.asarray(L0_ffn_A).transpose(1, 0, 2)
                                .reshape(D, 128))
    fb_r = np.ascontiguousarray(np.asarray(L0_ffn_B).reshape(128, D))
    sel16 = np.kron(np.eye(16, dtype=np.float32), np.ones((1, 64), np.float32))
    common = dict(
        sel16=sel16,
        wqk=wqk_p,
        wv=np.ascontiguousarray(L0_attn_W_v),
        wo=np.ascontiguousarray(L0_attn_W_out),
        wg=np.ascontiguousarray(L0_ffn_Wg),
        fa=fa_r, fb=fb_r,
        qs=np.ascontiguousarray(np.asarray(L0_attn_q_scale).reshape(1, DH)),
        qb=np.ascontiguousarray(np.asarray(L0_attn_q_bias).reshape(1, DH)),
        ks=np.ascontiguousarray(np.asarray(L0_attn_k_scale).reshape(1, DH)),
        kb=np.ascontiguousarray(np.asarray(L0_attn_k_bias).reshape(1, DH)),
    )
    in_maps = []
    for c in range(8):
        m = dict(common)
        m["x"] = np.ascontiguousarray(np.asarray(L0_attn_inputs)[c])
        in_maps.append(m)
    return in_maps


def kernel(**inputs):
    in_maps = make_inmaps(**inputs)
    nc = build_core()
    res = run_bass_kernel_spmd(nc, in_maps, core_ids=list(range(8)))
    return np.stack([np.asarray(r["o"], np.float32) for r in res.results],
                    axis=0)


# version 11
# speedup vs baseline: 1.0497x; 1.0497x over previous
import numpy as np
import concourse.bass as bass
import concourse.mybir as mybir
from concourse.bass_utils import run_bass_kernel_spmd

F32 = mybir.dt.float32
BF16 = mybir.dt.bfloat16
AF = mybir.ActivationFunctionType
ALU = mybir.AluOpType
AX = mybir.AxisListType

L = 2048
D = 1024
H = 16
DH = 64
NT = 16
LN32 = -3.4657359027997265


class Sem:
    __slots__ = ("h", "n")

    def __init__(self, h):
        self.h = h
        self.n = 0

    def inc(self, k=1):
        self.n += k
        return self.n


class B:

    def __init__(self, nc):
        self.nc = nc
        self.ops = []
        self.cms = {}
        self.waited = {}

    def wait(self, eng, sem, val):
        if not val or val <= 0:
            return
        key = (eng, id(sem))
        if self.waited.get(key, 0) >= val:
            return
        self.waited[key] = val
        self.ops.append((eng, "w", (sem.h, val)))

    def op(self, eng, name, args, kwargs=None, inc=None, inck=1):
        n = inc.inc(inck) if inc is not None else None
        self.ops.append((eng, "o", (name, args, kwargs or {},
                                    inc.h if inc else None, inck)))
        return n

    def sb(self, name, shape, dtype, side="left"):
        cm = self.nc.sbuf_tensor(name, shape, dtype, side=side)
        h = cm.__enter__()
        self.cms[name] = cm
        return h

    def ps(self, name, shape, dtype):
        cm = self.nc.psum_tensor(name, shape, dtype)
        h = cm.__enter__()
        self.cms[name] = cm
        return h

    def free(self, *names):
        for n in names:
            self.cms.pop(n).__exit__(None, None, None)

    def replay(self, eng_obj, eng):
        for e, kind, p in self.ops:
            if e != eng:
                continue
            if kind == "w":
                eng_obj.wait_ge(*p)
            else:
                name, args, kwargs, sem, inck = p
                inst = getattr(eng_obj, name)(*args, **kwargs)
                if sem is not None:
                    inst.then_inc(sem, inck)


def build_core(dbg=(), upto="full"):
    nc = bass.Bass("TRN2", target_bir_lowering=False)
    bd = B(nc)
    dbg = set(dbg)

    x_h = nc.dram_tensor("x", [L, D], F32, kind="ExternalInput")
    wqk_h = nc.dram_tensor("wqk", [D, 2 * D], F32, kind="ExternalInput")
    wv_h = nc.dram_tensor("wv", [D, D], F32, kind="ExternalInput")
    wo_h = nc.dram_tensor("wo", [D, D], F32, kind="ExternalInput")
    wg_h = nc.dram_tensor("wg", [D, 8], F32, kind="ExternalInput")
    fa_h = nc.dram_tensor("fa", [D, 128], F32, kind="ExternalInput")
    fb_h = nc.dram_tensor("fb", [128, D], F32, kind="ExternalInput")
    qs_h = nc.dram_tensor("qs", [1, DH], F32, kind="ExternalInput")
    qb_h = nc.dram_tensor("qb", [1, DH], F32, kind="ExternalInput")
    ks_h = nc.dram_tensor("ks", [1, DH], F32, kind="ExternalInput")
    kb_h = nc.dram_tensor("kb", [1, DH], F32, kind="ExternalInput")
    sel_h = nc.dram_tensor("sel16", [16, 1024], F32, kind="ExternalInput")
    o_h = nc.dram_tensor("o", [L, D], F32, kind="ExternalOutput")
    x_ap = x_h.ap()
    o_ap = o_h.ap()
    dbg_d = {}
    for name, shape, dt in [("qkn", [128, NT * 2048], BF16),
                            ("vbuf", [128, NT * 1040], BF16),
                            ("aot", [128, 8 * 2048], BF16),
                            ("x1", [128, NT * 1024], F32),
                            ("lnT", [128, 8 * 2048], BF16)]:
        if name in dbg:
            dbg_d[name] = nc.dram_tensor("dbg_" + name, shape, dt,
                                         kind="ExternalOutput").ap()

    with (
        nc.semaphore("tt") as _tt,
        nc.semaphore("vv") as _vv,
        nc.semaphore("aa") as _aa,
        nc.semaphore("gg") as _gg,
        nc.semaphore("sy") as _sy,
        nc.semaphore("sc") as _sc,
        nc.semaphore("gq") as _gq,
        nc.Block() as block,
    ):
        tt, vv, aa, gg = Sem(_tt), Sem(_vv), Sem(_aa), Sem(_gg)
        sy, sc, gq = Sem(_sy), Sem(_sc), Sem(_gq)

        ident = bd.sb("ident", [128, 128], BF16)
        dmask = bd.sb("dmask", [128, 128], BF16)
        sel16 = bd.sb("sel16b", [16, 1024], BF16)
        s_qq = bd.sb("s_qq", [128, 1], F32)
        b_qq = bd.sb("b_qq", [128, 1], F32)
        s_kk = bd.sb("s_kk", [128, 1], F32)
        b_kk = bd.sb("b_kk", [128, 1], F32)
        eps5 = bd.sb("eps5", [128, 1], F32)
        eps6 = bd.sb("eps6", [128, 1], F32)
        ln32t = bd.sb("ln32t", [128, 1], F32)
        dnraw = bd.sb("dnraw", [1, 2048], BF16)
        dn_tok = bd.sb("dn_tok", [128, 256], BF16)
        dn_r = bd.sb("dn_r", [128, 256], F32)
        dn_rb = bd.sb("dn_rb", [128, 256], BF16)
        dn_r16 = bd.sb("dn_r16", [16, 2048], BF16)
        def sb2(name, shape, dtype):
            return [bd.sb(name + "A", shape, dtype),
                    bd.sb(name + "B", shape, dtype)]
        l_sum = sb2("l_sum", [128, 1], F32)
        l_s2 = sb2("l_s2", [128, 1], F32)
        l_m = sb2("l_m", [128, 1], F32)
        l_es = sb2("l_es", [128, 1], F32)
        l_m2 = sb2("l_m2", [128, 1], F32)
        l_var = sb2("l_var", [128, 1], F32)
        l_lnv = sb2("l_lnv", [128, 1], F32)
        l_rstd = sb2("l_rstd", [128, 1], F32)
        l_nmr = sb2("l_nmr", [128, 1], F32)
        st_sum = sb2("st_sum", [128, 16], F32)
        st_sq = sb2("st_sq", [128, 16], F32)
        st_m = sb2("st_m", [128, 16], F32)
        st_es = sb2("st_es", [128, 16], F32)
        st_m2 = sb2("st_m2", [128, 16], F32)
        st_var = sb2("st_var", [128, 16], F32)
        st_lnv = sb2("st_lnv", [128, 16], F32)
        st_r32 = sb2("st_r32", [128, 16], F32)
        st_nmr = sb2("st_nmr", [128, 16], F32)
        sqqk = sb2("sqqk2", [128, 1024], BF16)
        sqln_s = bd.sb("sqln", [128, 1024], F32)
        sqln2 = [sqln_s, sqln_s]
        glog = sb2("glog", [128, 8], F32)
        geq1 = sb2("geq1", [128, 8], F32)
        geq2 = sb2("geq2", [128, 8], F32)
        gt1 = sb2("gt1", [128, 8], F32)
        gmsk = sb2("gmsk", [128, 8], F32)
        gcw = sb2("gcw", [128, 8], F32)
        gc1 = sb2("gc1", [128, 8], F32)
        gc2 = sb2("gc2", [128, 8], F32)
        gm1 = sb2("gm1", [128, 1], F32)
        gm2 = sb2("gm2", [128, 1], F32)
        gd = sb2("gd", [128, 1], F32)
        ge = sb2("ge", [128, 1], F32)
        ge1 = sb2("ge1", [128, 1], F32)
        gw1 = sb2("gw1", [128, 1], F32)
        gw2 = sb2("gw2", [128, 1], F32)
        xbuf = bd.sb("xbuf", [128, 2 * 1024], F32)
        x_bf = bd.sb("x_bf", [128, 2 * 1024], BF16)
        lntok = bd.sb("lntok", [128, 2 * 1024], BF16)
        qkraw = bd.sb("qkraw", [128, 2 * 1024], BF16)

        qkr = bd.sb("qkr", [128, 2 * 1024], BF16)
        qkT_all = bd.sb("qkT_all", [128, NT * 2048], BF16)
        vbuf = bd.sb("vbuf", [128, NT * 1040], BF16)
        lnT = bd.sb("lnT", [128, 8 * 2048], BF16, side="right")
        wqh = bd.sb("wqh", [128, 8 * 1024], BF16, side="right")
        wvb = bd.sb("wvb", [128, 8 * 1024], BF16, side="right")

        ptpA = bd.ps("ptpA", [128, 1024], BF16)
        ptpB = bd.ps("ptpB", [128, 1024], BF16)
        pqkA = bd.ps("pqkA", [128, 1024], F32)
        pqkB = bd.ps("pqkB", [128, 1024], F32)
        pv = bd.ps("pv", [128, 1024], F32)
        ptp = [ptpA, ptpB]

        bd.op("g", "memset", (ident[:], 0.0))
        bd.op("g", "affine_select", (), dict(
            out=ident[:], in_=ident[:], compare_op=ALU.not_equal, fill=1.0,
            base=0, pattern=[[-1, 128]], channel_multiplier=1))
        bd.op("g", "memset", (dmask[:], 1.0))
        bd.op("g", "affine_select", (), dict(
            out=dmask[:], in_=dmask[:], compare_op=ALU.is_ge, fill=0.0,
            base=0, pattern=[[1, 128]], channel_multiplier=-1))
        bd.op("g", "dma_start", (sel16[:], sel_h.ap()), inc=gq, inck=16)
        bd.op("g", "memset", (eps5[:], 1e-5))
        bd.op("g", "memset", (eps6[:], 1e-6))
        bd.op("g", "memset", (ln32t[:], LN32))
        bd.op("g", "memset",
              (bass.AP(vbuf, 64, [[16640, 128], [1040, 16], [65, 16]]), 1024.0),
              inc=gg)
        g_setup = gg.n

        def wq_src(col_lo):
            return bass.AP(wqk_h, col_lo,
                           [[2048, 128], [2048 * 128, 8], [1, 1024]])
        wdst8 = bass.AP(wqh, 0, [[8192, 128], [1024, 8], [1, 1024]])
        bd.op("g", "dma_start", (wdst8, wq_src(0)), inc=gq, inck=16)
        d_wq_q = gq.n
        bd.op("g", "dma_start",
              (bass.AP(wvb, 0, [[8192, 128], [1024, 8], [1, 1024]]),
               bass.AP(wv_h, 0, [[1024, 128], [1024 * 128, 8], [1, 1024]])),
              inc=gq, inck=16)
        d_wv = gq.n
        for vh, dst in [(qs_h, s_qq), (qb_h, b_qq), (ks_h, s_kk), (kb_h, b_kk)]:
            bd.op("g", "dma_start", (dst[0:64, 0:1], vh.ap()), inc=gq, inck=16)
            bd.op("g", "dma_start", (dst[64:128, 0:1], vh.ap()), inc=gq, inck=16)

        tpc = [0, []]
        lntok_tp = [0, 0]
        d_x = [0] * NT
        xbf_vv = [0] * NT
        xsq_aa = [0] * NT
        ln_red = [0] * NT
        ln_var = [0] * NT
        ln_rstd = [0] * NT
        ln_nvv = [0] * NT
        lnT_ev = [0] * NT

        def ln_s0(t, src, is_x1):
            par = t % 2
            if not is_x1:
                if t >= 2:
                    bd.wait("s", aa, xbf_vv[t - 2])
                    bd.wait("s", aa, xsq_aa[t - 2])
                bd.op("s", "dma_start",
                      (xbuf[:, par * 1024:(par + 1) * 1024],
                       x_ap[t * 128:(t + 1) * 128, :]), inc=sy, inck=16)
                d_x[t] = sy.n
                bd.wait("v", sy, d_x[t])
                bd.wait("a", sy, d_x[t])
            bd.op("v", "tensor_reduce", (l_sum[par][:], src, AX.X, ALU.add),
                  inc=vv)
            ln_red[t] = vv.n
            bd.op("a", "activation", (sqln2[par][:], src, AF.Square),
                  dict(accum_out=l_s2[par][:]), inc=aa)
            if t >= 2 and ln_nvv[t - 2]:
                bd.wait("a", vv, ln_nvv[t - 2])
            bd.op("a", "activation",
                  (x_bf[:, par * 1024:(par + 1) * 1024], src, AF.Copy), inc=aa)
            xsq_aa[t] = aa.n
            xbf_vv[t] = aa.n

        def ln_s1(t, eps_ap):
            par = t % 2
            bd.wait("v", aa, xsq_aa[t])
            bd.wait("v", vv, ln_red[t])
            bd.op("v", "tensor_scalar",
                  (l_m[par][:], l_sum[par][:], 1.0 / 1024, None),
                  dict(op0=ALU.mult), inc=vv)
            bd.wait("v", vv, vv.n)
            bd.op("v", "tensor_scalar",
                  (l_es[par][:], l_s2[par][:], 1.0 / 1024, None),
                  dict(op0=ALU.mult), inc=vv)
            bd.wait("v", vv, vv.n)
            bd.op("v", "tensor_scalar",
                  (l_m2[par][:], l_m[par][:], l_m[par][:], None),
                  dict(op0=ALU.mult), inc=vv)
            bd.wait("v", vv, vv.n)
            bd.op("v", "tensor_tensor", (l_var[par][:], l_es[par][:], l_m2[par][:]),
                  dict(op=ALU.subtract), inc=vv)
            ln_var[t] = vv.n
            bd.wait("a", vv, ln_var[t])
            bd.op("a", "activation", (l_lnv[par][:], l_var[par][:], AF.Ln),
                  dict(bias=eps_ap), inc=aa)
            bd.wait("a", aa, aa.n)
            bd.op("a", "activation", (l_rstd[par][:], l_lnv[par][:], AF.Exp),
                  dict(scale=-0.5), inc=aa)
            ln_rstd[t] = aa.n

        def ln_s2(t, lnT_dst):
            par = t % 2
            bd.wait("v", aa, ln_rstd[t])
            bd.op("v", "tensor_scalar",
                  (l_nmr[par][:], l_m[par][:], l_rstd[par][:], -1.0),
                  dict(op0=ALU.mult, op1=ALU.mult), inc=vv)
            bd.wait("v", vv, vv.n)
            bd.wait("v", tt, lntok_tp[par])
            bd.op("v", "tensor_scalar",
                  (lntok[:, par * 1024:(par + 1) * 1024],
                   x_bf[:, par * 1024:(par + 1) * 1024],
                   l_rstd[par][:], l_nmr[par][:]),
                  dict(op0=ALU.mult, op1=ALU.add), inc=vv)
            v_ln = vv.n
            ln_nvv[t] = v_ln
            bd.wait("t", vv, v_ln)
            for c in range(8):
                p = ptp[tpc[0] % 2]
                if tpc[0] >= 2:
                    ge_, gn_ = tpc[1][tpc[0] - 2]
                    bd.wait("t", vv if ge_ == "v" else aa, gn_)
                bd.op("t", "transpose",
                      (p[:, 0:128],
                       lntok[:, par * 1024 + c * 128: par * 1024 + c * 128 + 128],
                       ident[:]), inc=tt)
                ntp = tt.n
                bd.wait("a", tt, ntp)
                bd.op("a", "activation",
                      (lnT_dst[:, c * 2048 + t * 128: c * 2048 + t * 128 + 128],
                       p[:, 0:128], AF.Copy), inc=aa)
                tpc[1].append(("a", aa.n))
                tpc[0] += 1
            lntok_tp[par] = tt.n
            lnT_ev[t] = ("a", aa.n)

        bd.wait("t", gg, g_setup)
        bd.wait("v", gg, g_setup)
        bd.wait("a", gg, g_setup)
        for i in range(NT + 2):
            if i >= 2:
                ln_s2(i - 2, lnT)
            if 1 <= i <= NT:
                ln_s1(i - 1, eps5[:])
            if i < NT:
                ln_s0(i, xbuf[:, (i % 2) * 1024:(i % 2 + 1) * 1024], False)

        qkn_done = [0] * (2 * NT)
        qkraw_free = [None, None]
        qkr_tp = [0, 0]
        pq_free = {0: [], 1: []}
        vbuf_ev = [0] * NT
        pv_free = []
        u_mm = {}
        u_araw = {}
        u_r32 = {}

        units = ([("q", t) for t in range(NT)] + [("v", t) for t in range(NT)]
                 + [("k", t) for t in range(NT)])

        def proj_m(u):
            kind, t = u
            if kind == "v":
                bd.wait("t", gq, d_wv)
                bd.wait("t", aa, lnT_ev[t][1])
                for sem, val in pv_free:
                    bd.wait("t", sem, val)
                wsrc, ps_ = wvb, pv
            else:
                bd.wait("t", gq, d_wq_q if kind == "q" else d_wq_k_val[0])
                bd.wait("t", aa, lnT_ev[t][1])
                for sem, val in pq_free[t % 2]:
                    bd.wait("t", sem, val)
                wsrc, ps_ = wqh, (pqkA if t % 2 == 0 else pqkB)
            for f in range(8):
                for cc in range(2):
                    bd.op("t", "matmul",
                          (ps_[:, cc * 512:(cc + 1) * 512],
                           lnT[:, f * 2048 + t * 128: f * 2048 + t * 128 + 128],
                           wsrc[:, f * 1024 + cc * 512: f * 1024 + (cc + 1) * 512]),
                          dict(start=(f == 0), stop=(f == 7)),
                          inc=(tt if (f == 7 and cc == 1) else None))
            u_mm[u] = tt.n

        def proj_s1(u):
            kind, t = u
            par = t % 2
            if kind == "v":
                bd.wait("v", tt, u_mm[u])
                bd.op("v", "tensor_copy",
                      (bass.AP(vbuf, t * 1040, [[16640, 128], [65, 16], [1, 64]]),
                       pv[:, 0:1024].rearrange("p (g d) -> p g d", d=64)), inc=vv)
                vbuf_ev[t] = vv.n
                pv_free[:] = [(vv, vv.n)]
                return
            half = 0 if kind == "q" else 1
            ps_ = pqkA if par == 0 else pqkB
            bd.wait("a", tt, u_mm[u])
            qf = qkraw_free[par]
            if qf:
                bd.wait("a", aa, qf[1])
                bd.wait("a", vv, qf[2])
            bd.op("a", "activation",
                  (qkraw[:, par * 1024:(par + 1) * 1024], ps_[:, 0:1024],
                   AF.Copy), inc=aa)
            a_raw = aa.n
            u_araw[u] = a_raw
            bd.wait("v", tt, u_mm[u])
            bd.wait("v", aa, a_raw)
            bd.op("v", "tensor_reduce",
                  (st_sum[par][:],
                   ps_[:, 0:1024].rearrange("p (g d) -> p g d", d=64),
                   AX.X, ALU.add), inc=vv)
            v_sums = vv.n
            bd.op("a", "activation",
                  (sqqk[par][:], qkraw[:, par * 1024:(par + 1) * 1024],
                   AF.Square), inc=aa)
            a_sq2 = aa.n
            bd.wait("v", aa, a_sq2)
            bd.op("v", "tensor_reduce",
                  (st_sq[par][:],
                   sqqk[par][:].rearrange("p (g d) -> p g d", d=64),
                   AX.X, ALU.add), inc=vv)
            pq_free[par] = [(vv, v_sums), (aa, a_raw)]
            bd.wait("v", vv, vv.n)
            bd.op("v", "tensor_scalar",
                  (st_m[par][:], st_sum[par][:], 1.0 / 2048.0, None),
                  dict(op0=ALU.mult), inc=vv)
            bd.wait("v", vv, vv.n)
            bd.op("v", "tensor_scalar",
                  (st_es[par][:], st_sq[par][:], 1.0 / 65536.0, None),
                  dict(op0=ALU.mult), inc=vv)
            bd.wait("v", vv, vv.n)
            bd.op("v", "tensor_tensor",
                  (st_m2[par][:], st_m[par][:], st_m[par][:]),
                  dict(op=ALU.mult), inc=vv)
            bd.wait("v", vv, vv.n)
            bd.op("v", "tensor_tensor",
                  (st_var[par][:], st_es[par][:], st_m2[par][:]),
                  dict(op=ALU.subtract), inc=vv)
            v_var = vv.n
            bd.wait("a", vv, v_var)
            bd.op("a", "activation", (st_lnv[par][:], st_var[par][:], AF.Ln),
                  dict(bias=eps6[:]), inc=aa)
            bd.wait("a", aa, aa.n)
            bd.op("a", "activation", (st_r32[par][:], st_lnv[par][:], AF.Exp),
                  dict(scale=-0.5, bias=ln32t[:]), inc=aa)
            u_r32[u] = aa.n

        def proj_s2(u):
            kind, t = u
            if kind == "v":
                return
            par = t % 2
            half = 0 if kind == "q" else 1
            bd.wait("v", aa, u_r32[u])
            bd.op("v", "tensor_tensor",
                  (st_nmr[par][:], st_m[par][:], st_r32[par][:]),
                  dict(op=ALU.mult), inc=vv)
            bd.wait("v", vv, vv.n)
            bd.op("v", "tensor_scalar",
                  (st_nmr[par][:], st_nmr[par][:], -32.0, None),
                  dict(op0=ALU.mult), inc=vv)
            bd.wait("v", vv, vv.n)
            v_nmr = vv.n
            bd.wait("v", tt, qkr_tp[par])
            bd.wait("a", tt, qkr_tp[par])
            bd.wait("a", vv, v_nmr)
            for gix in range(8):
                bd.op("v", "tensor_scalar",
                      (qkr[:, par * 1024 + gix * 64: par * 1024 + gix * 64 + 64],
                       qkraw[:, par * 1024 + gix * 64: par * 1024 + gix * 64 + 64],
                       st_r32[par][:, gix:gix + 1], st_nmr[par][:, gix:gix + 1]),
                      dict(op0=ALU.mult, op1=ALU.add), inc=vv)
            for gix in range(8, 16):
                bd.op("a", "activation",
                      (qkr[:, par * 1024 + gix * 64: par * 1024 + gix * 64 + 64],
                       qkraw[:, par * 1024 + gix * 64: par * 1024 + gix * 64 + 64],
                       AF.Identity),
                      dict(scale=st_r32[par][:, gix:gix + 1],
                           bias=st_nmr[par][:, gix:gix + 1]), inc=aa)
            v_norm = vv.n
            a_norm = aa.n
            qkraw_free[par] = ("both", a_norm, v_norm)
            sv_ = s_qq if half == 0 else s_kk
            bv_ = b_qq if half == 0 else b_kk
            bd.wait("t", vv, v_norm)
            bd.wait("t", aa, a_norm)
            for g in range(8):
                p = ptp[tpc[0] % 2]
                if tpc[0] >= 2:
                    ge_, gn_ = tpc[1][tpc[0] - 2]
                    bd.wait("t", vv if ge_ == "v" else aa, gn_)
                bd.op("t", "transpose",
                      (p[:, 0:128],
                       qkr[:, par * 1024 + g * 128: par * 1024 + g * 128 + 128],
                       ident[:]), inc=tt)
                ntp = tt.n
                bd.wait("v", tt, ntp)
                bd.op("v", "tensor_scalar",
                      (qkT_all[:, (2 * g + half) * 2048 + t * 128:
                               (2 * g + half) * 2048 + t * 128 + 128],
                       p[:, 0:128], sv_[:], bv_[:]),
                      dict(op0=ALU.mult, op1=ALU.add), inc=vv)
                tpc[1].append(("v", vv.n))
                tpc[0] += 1
            qkr_tp[par] = tt.n
            qkn_done[t + half * NT] = ("v", aa.n, vv.n)

        d_wq_k_val = [0]
        for i in range(len(units) + 2):
            if i >= 2:
                proj_s2(units[i - 2])
            if 1 <= i <= len(units):
                proj_s1(units[i - 1])
            if i < len(units):
                if units[i] == ("k", 0):
                    bd.wait("g", tt, u_mm[("q", NT - 1)])
                    bd.op("g", "dma_start", (wdst8, wq_src(1024)),
                          inc=gq, inck=16)
                    d_wq_k_val[0] = gq.n
                proj_m(units[i])
        aa_passk = aa.n

        if "qkn" in dbg_d:
            bd.wait("s", vv, vv.n)
            bd.op("s", "dma_start", (dbg_d["qkn"], qkT_all[:]), inc=sy, inck=16)
        if "vbuf" in dbg_d:
            bd.wait("s", vv, vv.n)
            bd.op("s", "dma_start", (dbg_d["vbuf"], vbuf[:]), inc=sy, inck=16)
        if "lnT" in dbg_d:
            bd.wait("s", vv, vv.n)
            bd.op("s", "dma_start", (dbg_d["lnT"], lnT[:]), inc=sy, inck=16)

        if upto == "proj":
            bd.wait("s", sy, sy.n)
            bd.wait("g", sy, sy.n)
            return _finish(bd, block, nc)

        bd.free("wvb", "wqh", "lnT")
        bd.free("pv", "pqkB", "pqkA", "ptpB", "ptpA")
        aoT = bd.sb("aoT", [128, 8 * 2048], BF16, side="right")
        pTb = bd.sb("pTb", [128, 2 * 2048], BF16, side="right")
        repsb = bd.sb("repsb", [64, 2048], BF16, side="right")
        otmp = bd.sb("otmp", [64, 2048], BF16, side="right")
        repsb_hi = bd.sb("repsb_hi", [128, 2048], BF16, side="right")
        psA = bd.ps("psA", [128, 1024], F32)
        psB = bd.ps("psB", [128, 1024], F32)
        poutA = bd.ps("poutA", [128, 1024], F32)
        poutB = bd.ps("poutB", [128, 1024], F32)
        ps2 = [psA, psB]
        pout2 = [poutA, poutB]

        att_start_vv = vv.n
        aoT_parts = []
        pout_free_h = {0: [(vv, att_start_vv), (aa, aa_passk),
                           (vv, vbuf_ev[NT - 1])],
                       1: [(vv, att_start_vv), (aa, aa_passk),
                           (vv, vbuf_ev[NT - 1])]}
        ps_free_par = {0: [(vv, att_start_vv), (aa, aa_passk)],
                       1: [(vv, att_start_vv), (aa, aa_passk)]}
        exp_par = [0, 0]
        ptb_free = [0, 0, 0, 0]
        shift_prev = 0
        dnraw_dma = 0

        for g in range(8):
            hA, hB = 2 * g, 2 * g + 1
            qblk, kblk = 2 * g, 2 * g + 1
            for qh in range(2):
                qbase = 1024 * qh
                jmax = 8 * qh + 8
                pend = None
                av_last = 0

                def emit_av2(pend):
                    j, a0, n, off, aexps, mvvs = pend
                    last = 0
                    for hx in range(2):
                        h = 2 * g + hx
                        pout_ = pout2[hx]
                        bd.wait("t", aa, aexps[hx])
                        if mvvs[hx]:
                            bd.wait("t", vv, mvvs[hx])
                        if j == 0:
                            for sem, val in pout_free_h[hx]:
                                bd.wait("t", sem, val)
                            pout_free_h[hx] = []
                        w = off
                        while w < off + n:
                            we = min((w // 512 + 1) * 512, off + n)
                            bd.op("t", "matmul",
                                  (pout_[0:65, w:we],
                                   vbuf[:, j * 1040 + h * 65:
                                        j * 1040 + h * 65 + 65],
                                   pTb[:, (hx * 2 + j % 2) * 1024 + w:
                                        (hx * 2 + j % 2) * 1024 + we]),
                                  dict(start=(j == 0), stop=(j == jmax - 1)),
                                  inc=(tt if we == off + n else None))
                            w = we
                        ptb_free[hx * 2 + j % 2] = tt.n
                        last = tt.n
                    return last

                for j in range(jmax):
                    qlo_g = 128 * j
                    a0 = max(qbase, qlo_g)
                    n = qbase + 1024 - a0
                    off = a0 - qbase
                    diag = qlo_g >= qbase
                    aexps = [0, 0]
                    mvvs = [0, 0]
                    for hx in range(2):
                        base = hx * 64
                        ps_ = ps2[hx]
                        for sem, val in ps_free_par[hx]:
                            bd.wait("t", sem, val)
                        ps_free_par[hx] = []
                        if exp_par[hx]:
                            bd.wait("t", aa, exp_par[hx])
                        lhs_k = qkT_all[base:base + 64,
                                        kblk * 2048 + qlo_g:
                                        kblk * 2048 + qlo_g + 128]
                        w = off
                        while w < off + n:
                            we = min((w // 512 + 1) * 512, off + n)
                            bd.op("t", "matmul",
                                  (ps_[:, w:we], lhs_k,
                                   qkT_all[base:base + 64,
                                           qblk * 2048 + qbase + w:
                                           qblk * 2048 + qbase + we]),
                                  dict(start=True, stop=True),
                                  inc=(tt if we == off + n else None))
                            w = we
                        s_done = tt.n
                        preg = (hx * 2 + j % 2) * 1024
                        bd.wait("a", tt, s_done)
                        bd.wait("a", tt, ptb_free[hx * 2 + j % 2])
                        bd.op("a", "activation",
                              (pTb[:, preg + off: preg + off + n],
                               ps_[:, off:off + n], AF.Exp),
                              dict(scale=1.0 / 64), inc=aa)
                        aexps[hx] = aa.n
                        exp_par[hx] = aa.n
                        if diag:
                            bd.wait("v", aa, aexps[hx])
                            bd.op("v", "tensor_tensor",
                                  (pTb[:, preg + off: preg + off + 128],
                                   pTb[:, preg + off: preg + off + 128],
                                   dmask[:]), dict(op=ALU.mult), inc=vv)
                            mvvs[hx] = vv.n
                    if pend is not None:
                        av_last = emit_av2(pend)
                    pend = (j, a0, n, off, aexps, mvvs)
                av_last = emit_av2(pend)

                bd.wait("a", tt, av_last)
                for hx in range(2):
                    h = 2 * g + hx
                    pout_ = pout2[hx]
                    if hx == 0:
                        raw_dst = aoT[0:64, g * 2048 + qbase:
                                      g * 2048 + qbase + 1024]
                    else:
                        bd.wait("a", sy, shift_prev)
                        raw_dst = otmp[0:64, 0:1024]
                    bd.wait("a", tt, av_last)
                    bd.op("a", "activation",
                          (raw_dst, pout_[0:64, 0:1024], AF.Copy), inc=aa)
                    a_raw0 = aa.n
                    bd.wait("a", sc, dnraw_dma)
                    bd.op("a", "activation",
                          (dnraw[0:1, 0:1024], pout_[64:65, 0:1024], AF.Copy),
                          inc=aa)
                    a_dn = aa.n
                    pout_free_h[hx] = [(aa, a_dn)]
                    bd.wait("a", aa, a_dn)
                    bd.op("a", "dma_start",
                          (dn_tok[8 * h + 4 * qh: 8 * h + 4 * qh + 4, :],
                           dnraw[0:1, 0:1024]), inc=sc, inck=16)
                    dnraw_dma = sc.n
                    if hx == 0:
                        aoT_parts.append((aa, a_raw0))
                    else:
                        bd.wait("s", aa, a_raw0)
                        bd.op("s", "dma_start",
                              (aoT[64:128, g * 2048 + qbase:
                                   g * 2048 + qbase + 1024],
                               otmp[0:64, 0:1024]), inc=sy, inck=16)
                        shift_prev = sy.n
                        aoT_parts.append((sy, sy.n))

        att_tt = tt.n

        bd.wait("v", sc, dnraw_dma)
        bd.op("v", "reciprocal", (dn_r[:], dn_tok[:]), inc=vv)
        bd.wait("v", vv, vv.n)
        bd.op("v", "tensor_copy", (dn_rb[:], dn_r[:]), inc=vv)
        v_dnrb = vv.n
        bd.wait("a", vv, v_dnrb)
        for h in range(H):
            bd.op("a", "dma_start",
                  (dn_r16[h:h + 1, :], dn_rb[8 * h: 8 * h + 8, :]),
                  inc=sc, inck=16)
        d_r16 = sc.n
        norm_done = []
        rep_free = []
        shift2_prev = 0
        for h in range(H):
            g = h // 2
            bd.wait("t", sc, d_r16)
            if h == 0:
                for kp in (0, 1):
                    for sem, val in ps_free_par[kp]:
                        bd.wait("t", sem, val)
                    if exp_par[kp]:
                        bd.wait("t", aa, exp_par[kp])
            for sem, val in rep_free:
                bd.wait("t", sem, val)
            rep_free = []
            for r in range(2):
                for w in range(2):
                    bd.op("t", "matmul",
                          (psA[0:64, w * 512:(w + 1) * 512],
                           sel16[0:16, 64 * h: 64 * h + 64],
                           dn_r16[0:16, (2 * r + w) * 512:(2 * r + w + 1) * 512]),
                          dict(start=True, stop=True),
                          inc=(tt if w == 1 else None))
                t_rep = tt.n
                bd.wait("v", tt, t_rep)
                if h % 2 == 1:
                    bd.wait("v", sy, shift2_prev)
                bd.op("v", "tensor_copy",
                      (repsb[0:64, r * 1024:(r + 1) * 1024], psA[0:64, 0:1024]),
                      inc=vv)
                v_reps = vv.n
                if r == 0:
                    bd.wait("t", vv, v_reps)
            rep_free = [(vv, v_reps)]
            if h % 2 == 0:
                bd.wait("v", aa, aoT_parts[h][1])
                bd.op("v", "tensor_tensor",
                      (aoT[0:64, g * 2048:(g + 1) * 2048],
                       aoT[0:64, g * 2048:(g + 1) * 2048],
                       repsb[0:64, 0:2048]), dict(op=ALU.mult), inc=vv)
                norm_done.append((vv, vv.n))
            else:
                bd.wait("s", vv, v_reps)
                if len(norm_done) >= 2:
                    bd.wait("s", norm_done[-2][0], norm_done[-2][1])
                bd.op("s", "dma_start",
                      (repsb_hi[64:128, 0:2048], repsb[0:64, 0:2048]),
                      inc=sy, inck=16)
                shift2_prev = sy.n
                bd.wait("v", sy, shift2_prev)
                bd.wait("v", sy, aoT_parts[h][1])
                bd.op("v", "tensor_tensor",
                      (aoT[64:128, g * 2048:(g + 1) * 2048],
                       aoT[64:128, g * 2048:(g + 1) * 2048],
                       repsb_hi[64:128, 0:2048]), dict(op=ALU.mult), inc=vv)
                norm_done.append((vv, vv.n))
                rep_free.append((sy, shift2_prev))
        aoT_parts = norm_done
        att_vv = vv.n

        if upto == "attn":
            if "aot" in dbg_d:
                for sem, val in aoT_parts:
                    bd.wait("s", sem, val)
                bd.op("s", "dma_start", (dbg_d["aot"], aoT[:]), inc=sy, inck=16)
            bd.wait("s", sy, sy.n)
            bd.wait("g", sy, sy.n)
            return _finish(bd, block, nc)

        bd.free("repsb_hi", "otmp", "repsb", "pTb", "vbuf", "qkT_all")
        bd.free("poutB", "poutA", "psB", "psA")
        wob = bd.sb("wob", [128, 8 * 1024], BF16, side="right")
        x1 = bd.sb("x1", [128, NT * 1024], F32)
        poA = bd.ps("poA", [128, 1024], F32)
        poB = bd.ps("poB", [128, 1024], F32)

        bd.wait("g", tt, att_tt)
        bd.wait("g", vv, att_vv)
        bd.wait("g", sy, max(shift_prev, shift2_prev))
        bd.op("g", "dma_start",
              (bass.AP(wob, 0, [[8192, 128], [1024, 8], [1, 1024]]),
               bass.AP(wo_h, 0, [[1024, 128], [1024 * 128, 8], [1, 1024]])),
              inc=gq, inck=16)
        d_wo = gq.n

        for sem, val in aoT_parts:
            bd.wait("t", sem, val)
        bd.wait("t", gq, d_wo)
        bd.wait("t", aa, aa.n)
        x1_done = [0] * NT
        po_free = {0: [], 1: []}
        d_x2 = [0] * NT
        for t in range(NT):
            po = poA if t % 2 == 0 else poB
            if t >= 2:
                bd.wait("s", vv, x1_done[t - 2])
            bd.op("s", "dma_start",
                  (xbuf[:, (t % 2) * 1024:(t % 2 + 1) * 1024],
                   x_ap[t * 128:(t + 1) * 128, :]), inc=sy, inck=16)
            d_x2[t] = sy.n
            for sem, val in po_free[t % 2]:
                bd.wait("t", sem, val)
            for f in range(8):
                for cc in range(2):
                    bd.op("t", "matmul",
                          (po[:, cc * 512:(cc + 1) * 512],
                           aoT[:, f * 2048 + t * 128: f * 2048 + t * 128 + 128],
                           wob[:, f * 1024 + cc * 512: f * 1024 + (cc + 1) * 512]),
                          dict(start=(f == 0), stop=(f == 7)),
                          inc=(tt if (f == 7 and cc == 1) else None))
            t_po = tt.n
            bd.wait("v", tt, t_po)
            bd.wait("v", sy, d_x2[t])
            bd.op("v", "tensor_tensor",
                  (x1[:, t * 1024:(t + 1) * 1024], po[:, 0:1024],
                   xbuf[:, (t % 2) * 1024:(t % 2 + 1) * 1024]),
                  dict(op=ALU.add), inc=vv)
            x1_done[t] = vv.n
            po_free[t % 2] = [(vv, vv.n)]
        t_po_last = tt.n

        if "x1" in dbg_d:
            bd.wait("s", vv, x1_done[NT - 1])
            bd.op("s", "dma_start", (dbg_d["x1"], x1[:]), inc=sy, inck=16)
        if "aot" in dbg_d:
            for sem, val in aoT_parts:
                bd.wait("s", sem, val)
            bd.op("s", "dma_start", (dbg_d["aot"], aoT[:]), inc=sy, inck=16)

        bd.free("wob", "aoT")
        bd.free("poB", "poA")
        ptpA2 = bd.ps("ptpA2", [128, 1024], BF16)
        ptpB2 = bd.ps("ptpB2", [128, 1024], BF16)
        ptp[0] = ptpA2
        ptp[1] = ptpB2
        lnT2 = bd.sb("lnT2", [128, 8 * 2048], BF16, side="right")
        wgb = bd.sb("wgb", [128, 64], BF16, side="right")
        fab = bd.sb("fab", [128, 8 * 128], BF16, side="right")
        fbb = bd.sb("fbb", [128, 1024], BF16, side="right")
        hsbA = bd.sb("hsbA", [128, 128], BF16, side="right")
        hsbB = bd.sb("hsbB", [128, 128], BF16, side="right")
        hsb = [hsbA, hsbB]
        hwb = bd.sb("hwb", [128, 128], BF16, side="right")
        hwT = bd.sb("hwT", [128, 128], BF16, side="right")
        outst = bd.sb("outst", [128, 2 * 1024], F32, side="right")
        pg = bd.ps("pg", [128, 512], F32)
        ph = bd.ps("ph", [128, 512], F32)
        pfoA = bd.ps("pfoA", [128, 1024], F32)
        pfoB = bd.ps("pfoB", [128, 1024], F32)

        bd.wait("g", tt, t_po_last)
        bd.op("g", "dma_start",
              (bass.AP(wgb, 0, [[64, 128], [8, 8], [1, 8]]),
               bass.AP(wg_h, 0, [[8, 128], [1024, 8], [1, 8]])),
              inc=gq, inck=16)
        bd.op("g", "dma_start",
              (bass.AP(fab, 0, [[1024, 128], [128, 8], [1, 128]]),
               bass.AP(fa_h, 0, [[128, 128], [128 * 128, 8], [1, 128]])),
              inc=gq, inck=16)
        bd.op("g", "dma_start", (fbb[:], fb_h.ap()), inc=gq, inck=16)
        d_ffnw = gq.n

        gate_free = []
        pfo_free = {0: [], 1: []}
        hw_guard = [0]
        d_out = [0] * NT
        t_pg_l = [0] * NT
        t_ph_l = [0] * NT
        a_gl_l = [0] * NT
        a_ge_l = [0] * NT
        a_ph_l = [0] * NT
        gd_vv_l = [0] * NT
        hwb_vv_l = [0] * NT
        bd.wait("t", vv, x1_done[NT - 1])

        def ffn_f0(t):
            bd.wait("v", vv, x1_done[t])
            bd.wait("a", vv, x1_done[t])
            ln_s0(t, x1[:, t * 1024:(t + 1) * 1024], True)

        def ffn_f3(t):
            gp = t % 2
            bd.wait("t", gq, d_ffnw)
            bd.wait("t", aa, lnT_ev[t][1])
            if t >= 1:
                bd.wait("t", aa, a_gl_l[t - 1])
                bd.wait("t", aa, a_ph_l[t - 1])
            for f in range(8):
                bd.op("t", "matmul",
                      (pg[:, 0:8],
                       lnT2[:, f * 2048 + t * 128: f * 2048 + t * 128 + 128],
                       wgb[:, f * 8: f * 8 + 8]),
                      dict(start=(f == 0), stop=(f == 7)),
                      inc=(tt if f == 7 else None))
            t_pg_l[t] = tt.n
            for f in range(8):
                bd.op("t", "matmul",
                      (ph[:, 0:128],
                       lnT2[:, f * 2048 + t * 128: f * 2048 + t * 128 + 128],
                       fab[:, f * 128: f * 128 + 128]),
                      dict(start=(f == 0), stop=(f == 7)),
                      inc=(tt if f == 7 else None))
            t_ph_l[t] = tt.n

        def ffn_f4a(t):
            gp = t % 2
            bd.wait("a", tt, t_pg_l[t])
            if t >= 2:
                bd.wait("a", vv, gd_vv_l[t - 2])
            bd.op("a", "activation", (glog[gp][:], pg[:, 0:8], AF.Copy), inc=aa)
            a_gl_l[t] = aa.n
            bd.wait("a", tt, t_ph_l[t])
            if t >= 2:
                bd.wait("a", vv, hwb_vv_l[t - 2])
            bd.op("a", "activation", (hsb[gp][:], ph[:, 0:128], AF.Copy), inc=aa)
            a_ph_l[t] = aa.n

        def ffn_f4b(t):
            gp = t % 2
            bd.wait("v", aa, a_gl_l[t])
            bd.op("v", "tensor_reduce", (gm1[gp][:], glog[gp][:], AX.X, ALU.max),
                  inc=vv)
            bd.wait("v", vv, vv.n)
            bd.op("v", "tensor_scalar", (geq1[gp][:], glog[gp][:], gm1[gp][:], None),
                  dict(op0=ALU.is_equal), inc=vv)
            bd.wait("v", vv, vv.n)
            bd.op("v", "tensor_scalar", (gt1[gp][:], geq1[gp][:], 1e9, None),
                  dict(op0=ALU.mult), inc=vv)
            bd.wait("v", vv, vv.n)
            bd.op("v", "tensor_tensor", (gmsk[gp][:], glog[gp][:], gt1[gp][:]),
                  dict(op=ALU.subtract), inc=vv)
            bd.wait("v", vv, vv.n)
            bd.op("v", "tensor_reduce", (gm2[gp][:], gmsk[gp][:], AX.X, ALU.max),
                  inc=vv)
            bd.wait("v", vv, vv.n)
            bd.op("v", "tensor_scalar", (geq2[gp][:], glog[gp][:], gm2[gp][:], None),
                  dict(op0=ALU.is_equal), inc=vv)
            bd.wait("v", vv, vv.n)
            bd.op("v", "tensor_tensor", (gd[gp][:], gm2[gp][:], gm1[gp][:]),
                  dict(op=ALU.subtract), inc=vv)
            v_gd = vv.n
            gd_vv_l[t] = v_gd
            bd.wait("a", vv, v_gd)
            bd.op("a", "activation", (ge[gp][:], gd[gp][:], AF.Exp), inc=aa)
            a_ge_l[t] = aa.n

        def ffn_f5(t):
            gp = t % 2
            bd.wait("v", aa, a_ge_l[t])
            bd.op("v", "tensor_scalar", (ge1[gp][:], ge[gp][:], 1.0, None),
                  dict(op0=ALU.add), inc=vv)
            bd.wait("v", vv, vv.n)
            bd.op("v", "reciprocal", (gw1[gp][:], ge1[gp][:]), inc=vv)
            bd.wait("v", vv, vv.n)
            bd.op("v", "tensor_scalar", (gw2[gp][:], gw1[gp][:], -1.0, 1.0),
                  dict(op0=ALU.mult, op1=ALU.add), inc=vv)
            bd.wait("v", vv, vv.n)
            bd.op("v", "tensor_scalar", (gc1[gp][:], geq1[gp][:], gw1[gp][:], None),
                  dict(op0=ALU.mult), inc=vv)
            bd.wait("v", vv, vv.n)
            bd.op("v", "tensor_scalar", (gc2[gp][:], geq2[gp][:], gw2[gp][:], None),
                  dict(op0=ALU.mult), inc=vv)
            bd.wait("v", vv, vv.n)
            bd.op("v", "tensor_tensor", (gcw[gp][:], gc1[gp][:], gc2[gp][:]),
                  dict(op=ALU.add), inc=vv)
            bd.wait("v", aa, a_ph_l[t])
            bd.wait("v", vv, vv.n)
            bd.wait("v", tt, hw_guard[0])
            for e in range(8):
                bd.op("v", "tensor_scalar",
                      (hwb[:, e * 16:(e + 1) * 16],
                       hsb[gp][:, e * 16:(e + 1) * 16],
                       gcw[gp][:, e:e + 1], None), dict(op0=ALU.mult), inc=vv)
            hwb_vv_l[t] = vv.n

        def ffn_f6(t):
            p = ptp[tpc[0] % 2]
            if tpc[0] >= 2:
                ge_, gn_ = tpc[1][tpc[0] - 2]
                bd.wait("t", vv if ge_ == "v" else aa, gn_)
            bd.wait("t", vv, hwb_vv_l[t])
            bd.op("t", "transpose", (p[:, 0:128], hwb[:], ident[:]), inc=tt)
            t_hwT = tt.n
            bd.wait("v", tt, t_hwT)
            bd.op("v", "tensor_copy", (hwT[:], p[:, 0:128]), inc=vv)
            v_hwT = vv.n
            tpc[1].append(("v", vv.n))
            tpc[0] += 1
            pfo = pfoA if t % 2 == 0 else pfoB
            bd.wait("t", vv, v_hwT)
            for sem, val in pfo_free[t % 2]:
                bd.wait("t", sem, val)
            for cc in range(2):
                bd.op("t", "matmul",
                      (pfo[:, cc * 512:(cc + 1) * 512], hwT[:],
                       fbb[:, cc * 512:(cc + 1) * 512]),
                      dict(start=True, stop=True),
                      inc=(tt if cc == 1 else None))
            t_fo = tt.n
            hw_guard[0] = t_fo
            bd.wait("v", tt, t_fo)
            if t >= 2:
                bd.wait("v", sy, d_out[t - 2])
            bd.op("v", "tensor_tensor",
                  (outst[:, (t % 2) * 1024:(t % 2 + 1) * 1024],
                   pfo[:, 0:1024], x1[:, t * 1024:(t + 1) * 1024]),
                  dict(op=ALU.add), inc=vv)
            v_out = vv.n
            pfo_free[t % 2] = [(vv, v_out)]
            bd.wait("s", vv, v_out)
            bd.op("s", "dma_start",
                  (o_ap[t * 128:(t + 1) * 128, :],
                   outst[:, (t % 2) * 1024:(t % 2 + 1) * 1024]),
                  inc=sy, inck=16)
            d_out[t] = sy.n

        for i in range(NT + 4):
            if i >= 4:
                ffn_f5(i - 4)
                ffn_f6(i - 4)
            if i >= 3 and i - 3 < NT:
                ffn_f4b(i - 3)
            if i >= 2 and i - 2 < NT:
                ln_s2(i - 2, lnT2)
                ffn_f3(i - 2)
                ffn_f4a(i - 2)
            if 1 <= i <= NT:
                ln_s1(i - 1, eps5[:])
            if i < NT:
                ffn_f0(i)

        bd.wait("s", sy, d_out[NT - 1])
        bd.wait("g", sy, d_out[NT - 1])
        return _finish(bd, block, nc)


def _finish(bd, block, nc):
    @block.tensor
    def _(e):
        bd.replay(e, "t")

    @block.vector
    def _(e):
        bd.replay(e, "v")

    @block.scalar
    def _(e):
        bd.replay(e, "a")

    @block.gpsimd
    def _(e):
        bd.replay(e, "g")

    @block.sync
    def _(e):
        bd.replay(e, "s")

    return nc


def make_inmaps(L0_attn_inputs, L0_attn_W_qk, L0_attn_W_v, L0_attn_W_out,
                L0_attn_q_scale, L0_attn_q_bias, L0_attn_k_scale,
                L0_attn_k_bias, L0_ffn_Wg, L0_ffn_A, L0_ffn_B):
    wqk_p = np.ascontiguousarray(
        np.asarray(L0_attn_W_qk).reshape(D, H, 2, DH).transpose(0, 2, 1, 3)
        .reshape(D, 2 * D))
    fa_r = np.ascontiguousarray(np.asarray(L0_ffn_A).transpose(1, 0, 2)
                                .reshape(D, 128))
    fb_r = np.ascontiguousarray(np.asarray(L0_ffn_B).reshape(128, D))
    sel16 = np.kron(np.eye(16, dtype=np.float32), np.ones((1, 64), np.float32))
    common = dict(
        sel16=sel16,
        wqk=wqk_p,
        wv=np.ascontiguousarray(L0_attn_W_v),
        wo=np.ascontiguousarray(L0_attn_W_out),
        wg=np.ascontiguousarray(L0_ffn_Wg),
        fa=fa_r, fb=fb_r,
        qs=np.ascontiguousarray(np.asarray(L0_attn_q_scale).reshape(1, DH)),
        qb=np.ascontiguousarray(np.asarray(L0_attn_q_bias).reshape(1, DH)),
        ks=np.ascontiguousarray(np.asarray(L0_attn_k_scale).reshape(1, DH)),
        kb=np.ascontiguousarray(np.asarray(L0_attn_k_bias).reshape(1, DH)),
    )
    in_maps = []
    for c in range(8):
        m = dict(common)
        m["x"] = np.ascontiguousarray(np.asarray(L0_attn_inputs)[c])
        in_maps.append(m)
    return in_maps


def kernel(**inputs):
    in_maps = make_inmaps(**inputs)
    nc = build_core()
    res = run_bass_kernel_spmd(nc, in_maps, core_ids=list(range(8)))
    return np.stack([np.asarray(r["o"], np.float32) for r in res.results],
                    axis=0)
